# version 02
# baseline (speedup 1.0000x reference)
import contextlib
import numpy as np
import ml_dtypes
import concourse.bass as bass
import concourse.mybir as mybir
from concourse.bass_utils import run_bass_kernel_spmd

F32 = mybir.dt.float32
BF16 = mybir.dt.bfloat16
AF = mybir.ActivationFunctionType
ALU = mybir.AluOpType
BF = ml_dtypes.bfloat16

D = 1024
NCH = 8
NB = 2
SEQ = 16384
NCORE = 8
EPS = 1e-5

ENGINES = ("sync", "gpsimd", "tensor", "vector", "scalar")
ENG_OF = {"y": "sync", "g": "gpsimd", "t": "tensor", "v": "vector", "s": "scalar"}


class Plan:
    def __init__(self, nc, st):
        self.nc = nc
        self.st = st
        self.ops = {e: [] for e in ENGINES}
        self.sems = {}
        self.cnt = {}

    def sem(self, name):
        if name not in self.sems:
            self.sems[name] = self.st.enter_context(self.nc.semaphore(name))
            self.cnt[name] = 0
        return self.sems[name]

    def add(self, eng, fn, inc=None, n=1):
        eng = ENG_OF.get(eng, eng)
        if inc is not None:
            s = self.sem(inc)
            self.cnt[inc] += n
            self.ops[eng].append(lambda e, fn=fn, s=s, n=n: fn(e).then_inc(s, n))
        else:
            self.ops[eng].append(fn)

    def wait(self, eng, name, val=None):
        eng = ENG_OF.get(eng, eng)
        s = self.sem(name)
        v = self.cnt[name] if val is None else val
        if v > 0:
            self.ops[eng].append(lambda e, s=s, v=v: e.wait_ge(s, v))

    def val(self, name):
        return self.cnt.get(name, 0)

    def drain(self, eng):
        eng = ENG_OF.get(eng, eng)
        self.ops[eng].append(lambda e: e.drain())

    def build(self):
        block = self.st.enter_context(self.nc.Block())
        for eng in ENGINES:
            ops = self.ops[eng]
            if not ops:
                continue

            def body(e, ops=ops):
                for fn in ops:
                    fn(e)

            getattr(block, eng)(body)


class MMQ:

    def __init__(self, pl, psums, free_sem="dv"):
        self.pl = pl
        self.psums = psums
        self.free = [("", 0)] * len(psums)
        self.i = 0

    def group(self, mms, tag="pe"):
        pl = self.pl
        ps = self.psums[self.i]
        snm, sval = self.free[self.i]
        if snm:
            pl.wait("t", snm, sval)
        n = len(mms)
        for q, (ofn, lhsT, rhs) in enumerate(mms):
            pl.add("t", lambda e, ofn=ofn, lhsT=lhsT, rhs=rhs, ps=ps, q=q, n=n:
                   e.matmul(ofn(ps), lhsT, rhs, start=(q == 0), stop=(q == n - 1)),
                   inc=(tag if q == n - 1 else None))
        idx = self.i
        self.i = (self.i + 1) % len(self.psums)
        return ps, pl.val(tag), idx

    def done(self, idx, sem_name, value):
        self.free[idx] = (sem_name, value)


def emit_ln_rows(pl, ps_stat, rowa, W, T, double, tag, kEPS=None, kM05=None):
    m1 = rowa[0:1, 0:W]
    var = rowa[0:1, T:T + W]
    r1 = rowa[0:1, 2 * T:2 * T + W]
    sc = rowa[0:1, 3 * T:3 * T + W]
    pl.add("v", lambda e: e.tensor_single_scalar(m1, ps_stat[0:1, 0:W], 1.0 / D, ALU.mult))
    pl.add("v", lambda e: e.tensor_single_scalar(sc, ps_stat[0:1, T:T + W], 1.0 / D, ALU.mult))
    pl.drain("v")
    pl.add("v", lambda e: e.tensor_tensor(var, m1, m1, ALU.mult))
    pl.drain("v")
    pl.add("v", lambda e: e.tensor_tensor(var, sc, var, ALU.subtract), inc=tag + "v")
    pl.wait("s", tag + "v")
    pl.add("s", lambda e: e.activation(r1, var, AF.Ln, bias=kEPS))
    pl.drain("s")
    pl.add("s", lambda e: e.activation(r1, r1, AF.Exp, scale=kM05), inc=tag + "s")
    if double:
        pl.wait("v", tag + "s")
        pl.add("v", lambda e: e.tensor_tensor(sc, r1, r1, ALU.mult))
        pl.drain("v")
        pl.add("v", lambda e: e.tensor_tensor(sc, var, sc, ALU.mult), inc=tag + "v")
        pl.wait("s", tag + "v")
        pl.add("s", lambda e: e.activation(sc, sc, AF.Ln, bias=kEPS))
        pl.drain("s")
        pl.add("s", lambda e: e.activation(sc, sc, AF.Exp, scale=kM05), inc=tag + "s")
        pl.wait("v", tag + "s")
        pl.add("v", lambda e: e.tensor_tensor(r1, r1, sc, ALU.mult))
        pl.drain("v")
    else:
        pl.wait("v", tag + "s")
    pl.add("v", lambda e: e.tensor_tensor(m1, m1, r1, ALU.mult))
    pl.drain("v")
    return r1, m1


def emit_ab_bcast(pl, rowhl, alpha, beta, ones_r, ps_ab, W, T, tag):
    hi = rowhl[0:1, 0:W]
    lo = rowhl[0:1, T:T + W]
    for (row, off) in ((alpha, 0), (beta, T)):
        pl.add("v", lambda e, row=row: e.tensor_copy(hi, row))
        pl.drain("v")
        pl.add("v", lambda e, row=row: e.tensor_tensor(lo, row, hi, ALU.subtract),
               inc=tag + "v")
        pl.wait("t", tag + "v")
        pl.add("t", lambda e, off=off: e.matmul(ps_ab[:, off:off + W], ones_r[0:1, :], hi,
                                                start=True, stop=False))
        pl.add("t", lambda e, off=off: e.matmul(ps_ab[:, off:off + W], ones_r[0:1, :], lo,
                                                start=False, stop=True),
               inc=tag + "t")
        pl.wait("v", tag + "t")


PA_COLS = {}
_c = 0
for _nm, _w in [("ln1g", 8), ("ln1b", 8), ("inb", 16), ("dw0", 8), ("dw1", 8),
                ("dw2", 8), ("dwb", 8), ("dtb", 8), ("bb", 8), ("cb", 8),
                ("alog", 8), ("dskip", 8), ("maskL", 1), ("maskR", 1),
                ("kEPS", 1), ("kM05", 1), ("kM1", 1), ("k1", 1)]:
    PA_COLS[_nm] = _c
    _c += _w
PA_NCOL = _c


def build_mamba_passA(TOK, T=256):
    NT = TOK // T
    W = NT + 1
    assert W <= T
    nc = bass.Bass("TRN2", target_bir_lowering=False)
    x_h = nc.dram_tensor("x", [128, NCH, TOK + 2], F32, kind="ExternalInput").ap()
    xbnd_h = nc.dram_tensor("xbnd", [128, NCH, W], F32, kind="ExternalInput").ap()
    inW_h = nc.dram_tensor("inW", [128, NCH, 2 * D], F32, kind="ExternalInput").ap()
    dtW_h = nc.dram_tensor("dtW", [128, NCH, D], F32, kind="ExternalInput").ap()
    bW_h = nc.dram_tensor("bW", [128, NCH, D], F32, kind="ExternalInput").ap()
    cW_h = nc.dram_tensor("cW", [128, NCH, D], F32, kind="ExternalInput").ap()
    cols_h = nc.dram_tensor("cols", [128, PA_NCOL], F32, kind="ExternalInput").ap()
    z_h = nc.dram_tensor("z", [128, NCH, TOK], BF16, kind="ExternalOutput").ap()
    C_h = nc.dram_tensor("C", [128, NCH, TOK], BF16, kind="ExternalOutput").ap()
    PS_h = nc.dram_tensor("PS", [128, 16], F32, kind="ExternalOutput").ap()

    with contextlib.ExitStack() as st:
        def sb(name, shape, dtype):
            return st.enter_context(nc.sbuf_tensor(name, shape, dtype))

        def pt(name, shape):
            return st.enter_context(nc.psum_tensor(name, shape, F32))

        inW = sb("inW_s", [128, NCH, 2 * D], BF16)
        dtW = sb("dtW_s", [128, NCH, D], BF16)
        bW = sb("bW_s", [128, NCH, D], BF16)
        cW = sb("cW_s", [128, NCH, D], BF16)
        cols = sb("cols_s", [128, PA_NCOL], F32)
        acol = sb("acol", [128, NCH], F32)
        ones_c = sb("ones_c", [128, 1], BF16)
        ones_r = sb("ones_r", [1, 128], BF16)
        zeros = sb("zeros", [128, T], F32)
        xsl = [sb(f"x{i}", [128, NCH, T], F32) for i in range(2)]
        xbf = sb("xbf", [128, NCH, T], BF16)
        sqbf = sb("sqbf", [128, NCH, T], BF16)
        nx2 = sb("nx2", [128, NCH, T], BF16)
        xp = sb("xp", [128, NCH, T + 2], F32)
        stash = sb("stash", [128, NCH, 1], F32)
        xbnd = sb("xbnd_s", [128, NCH, W], F32)
        xbndb = sb("xbndb", [128, NCH, W], BF16)
        xph = sb("xph", [128, NCH, W], F32)
        gate = sb("gate", [128, NCH, T], F32)
        mslt = sb("mslt", [128, NCH, T], F32)
        mbf = sb("mbf", [128, NCH, T], BF16)
        spx = sb("spx", [128, NCH, T], F32)
        scr = sb("scr", [128, NCH, T], F32)
        btu = sb("btu", [128, NCH, T], F32)
        ctt = sb("ctt", [128, NCH, T], F32)
        ycr = sb("ycr", [128, NCH, T], F32)
        zbf = sb("zbf", [128, NCH, T], BF16)
        Cbf = sb("Cbf", [128, NCH, T], BF16)
        state = sb("state", [128, NCH], F32)
        astate = sb("astate", [128, NCH], F32)
        rowa = sb("rowa", [1, 4 * T], F32)
        rowhl = sb("rowhl", [1, 2 * T], BF16)

        ps_stat = pt("ps_stat", [1, 2 * T])
        ps_ab = pt("ps_ab", [128, 2 * T])
        ps_mm = [pt(f"ps_mm{i}", [128, T]) for i in range(2)]
        ps_dt = pt("ps_dt", [128, T])
        ps_bt = pt("ps_bt", [128, T])
        ps_ct = pt("ps_ct", [128, T])
        ps_sA = pt("ps_sA", [128, 2 * T])

        pl = Plan(nc, st)

        def col(name, c=0):
            i = PA_COLS[name] + c
            return cols[:, i:i + 1]

        pl.add("g", lambda e: e.dma_start(out=inW[:], in_=inW_h[:]), inc="wld", n=16)
        pl.add("g", lambda e: e.dma_start(out=dtW[:], in_=dtW_h[:]), inc="wld", n=16)
        pl.add("g", lambda e: e.dma_start(out=bW[:], in_=bW_h[:]), inc="wld", n=16)
        pl.add("g", lambda e: e.dma_start(out=cW[:], in_=cW_h[:]), inc="wld", n=16)
        pl.add("g", lambda e: e.dma_start(out=cols[:], in_=cols_h[:]), inc="wld", n=16)
        pl.add("g", lambda e: e.dma_start(out=xbnd[:], in_=xbnd_h[:]), inc="wld", n=16)
        WLD = pl.val("wld")
        for eng in ("v", "s", "t"):
            pl.wait(eng, "wld", WLD)
        pl.add("v", lambda e: e.memset(ones_c[:], 1.0))
        pl.add("v", lambda e: e.memset(ones_r[:], 1.0))
        pl.add("v", lambda e: e.memset(zeros[:], 0.0))
        pl.add("v", lambda e: e.memset(state[:], 0.0))
        pl.drain("v")
        pl.add("v", lambda e: e.memset(astate[:], 1.0), inc="dv")
        INIT_V = pl.val("dv")
        pl.add("s", lambda e: e.activation(acol[:], cols[:, PA_COLS["alog"]:PA_COLS["alog"] + 8],
                                           AF.Exp))
        pl.drain("s")
        pl.add("s", lambda e: e.activation(acol[:], acol[:], AF.Copy, scale=col("kM1")), inc="ac")

        pl.add("v", lambda e: e.tensor_copy(xbndb[:], xbnd[:]), inc="dv")
        for c in range(NCH):
            pl.add("s", lambda e, c=c: e.activation(sqbf[:, c, 0:W], xbnd[:, c, :], AF.Square),
                   inc=("ac" if c == NCH - 1 else None))
        pl.wait("t", "dv")
        pl.wait("t", "ac")
        for c in range(NCH):
            pl.add("t", lambda e, c=c: e.matmul(ps_stat[0:1, 0:W], ones_c[:], xbndb[:, c, :],
                                                start=(c == 0), stop=(c == NCH - 1)))
        for c in range(NCH):
            pl.add("t", lambda e, c=c: e.matmul(ps_stat[0:1, T:T + W], ones_c[:], sqbf[:, c, 0:W],
                                                start=(c == 0), stop=(c == NCH - 1)),
                   inc=("pe" if c == NCH - 1 else None))
        pl.wait("v", "pe")
        alpha, beta = emit_ln_rows(pl, ps_stat, rowa, W, T, double=True, tag="r",
                                   kEPS=cols[0:1, PA_COLS["kEPS"]:PA_COLS["kEPS"] + 1],
                                   kM05=cols[0:1, PA_COLS["kM05"]:PA_COLS["kM05"] + 1])
        emit_ab_bcast(pl, rowhl, alpha, beta, ones_r, ps_ab, W, T, tag="b")
        for c in range(NCH):
            pl.add("v", lambda e, c=c: e.tensor_tensor(xbnd[:, c, :], xbnd[:, c, :],
                                                       ps_ab[:, 0:W], ALU.mult))
        pl.drain("v")
        for c in range(NCH):
            pl.add("v", lambda e, c=c: e.tensor_tensor(xbnd[:, c, :], xbnd[:, c, :],
                                                       ps_ab[:, T:T + W], ALU.subtract))
        pl.drain("v")
        for c in range(NCH):
            pl.add("v", lambda e, c=c: e.tensor_scalar(xbndb[:, c, :], xbnd[:, c, :],
                                                       col("ln1g", c), col("ln1b", c),
                                                       ALU.mult, ALU.add),
                   inc=("dv" if c == NCH - 1 else None))
        pl.wait("t", "dv")
        mq = MMQ(pl, ps_mm)
        for j in range(NCH):
            ps, pv, idx = mq.group([((lambda p, j=j: p[:, 0:W]),
                                     inW[:, k, 128 * j:128 * (j + 1)], xbndb[:, k, :])
                                    for k in range(NCH)])
            pl.wait("v", "pe", pv)
            pl.add("v", lambda e, ps=ps, j=j: e.tensor_scalar(
                xph[:, j, :], ps[:, 0:W], col("inb", j), None, ALU.add), inc="dv")
            mq.done(idx, "dv", pl.val("dv"))
        pl.drain("v")
        pl.add("v", lambda e: e.tensor_scalar(xph[:, :, 0:1], xph[:, :, 0:1],
                                              col("maskL"), None, ALU.mult))
        pl.add("v", lambda e: e.tensor_scalar(xph[:, :, W - 1:W], xph[:, :, W - 1:W],
                                              col("maskR"), None, ALU.mult), inc="dv")

        xfree = [("", 0), ("", 0)]

        def load_x(i):
            sl = i % 2
            snm, sval = xfree[sl]
            if snm:
                pl.wait("y", snm, sval)
            pl.add("y", lambda e, i=i, sl=sl: e.dma_start(
                out=xsl[sl][:], in_=x_h[:, :, 1 + i * T:1 + (i + 1) * T]),
                inc=f"xld{sl}", n=16)

        load_x(0)

        spfree = ("", 0)
        M_DONE_PREV = 0
        ZC_DONE_PREV = 0

        for i in range(NT):
            sl = i % 2
            xt = xsl[sl]
            pl.wait("v", f"xld{sl}")
            pl.wait("s", f"xld{sl}")
            pl.add("v", lambda e, xt=xt: e.tensor_copy(xbf[:], xt[:]), inc="dv")
            for c in range(NCH):
                pl.add("s", lambda e, xt=xt, c=c: e.activation(sqbf[:, c, :], xt[:, c, :], AF.Square),
                       inc=("ac" if c == NCH - 1 else None))
            pl.wait("t", "dv")
            pl.wait("t", "ac")
            for c in range(NCH):
                pl.add("t", lambda e, c=c: e.matmul(ps_stat[0:1, 0:T], ones_c[:], xbf[:, c, :],
                                                    start=(c == 0), stop=(c == NCH - 1)))
            for c in range(NCH):
                pl.add("t", lambda e, c=c: e.matmul(ps_stat[0:1, T:2 * T], ones_c[:], sqbf[:, c, :],
                                                    start=(c == 0), stop=(c == NCH - 1)),
                       inc=("pe" if c == NCH - 1 else None))
            pl.wait("v", "pe")
            alpha, beta = emit_ln_rows(pl, ps_stat, rowa, T, T, double=True, tag="r",
                                       kEPS=cols[0:1, PA_COLS["kEPS"]:PA_COLS["kEPS"] + 1],
                                       kM05=cols[0:1, PA_COLS["kM05"]:PA_COLS["kM05"] + 1])
            emit_ab_bcast(pl, rowhl, alpha, beta, ones_r, ps_ab, T, T, tag="b")
            for c in range(NCH):
                pl.add("v", lambda e, xt=xt, c=c: e.tensor_tensor(
                    xt[:, c, :], xt[:, c, :], ps_ab[:, 0:T], ALU.mult))
            pl.drain("v")
            for c in range(NCH):
                pl.add("v", lambda e, xt=xt, c=c: e.tensor_tensor(
                    xt[:, c, :], xt[:, c, :], ps_ab[:, T:2 * T], ALU.subtract))
            pl.drain("v")
            for c in range(NCH):
                pl.add("v", lambda e, xt=xt, c=c: e.tensor_scalar(
                    nx2[:, c, :], xt[:, c, :], col("ln1g", c), col("ln1b", c),
                    ALU.mult, ALU.add),
                    inc=("dv" if c == NCH - 1 else None))
            NX2_DONE = pl.val("dv")
            xfree[sl] = ("dv", NX2_DONE)
            if i + 1 < NT:
                load_x(i + 1)

            if i > 0:
                pl.wait("v", "pl", M_DONE_PREV)
            if i == 0:
                pl.add("v", lambda e: e.tensor_copy(xp[:, :, 0:1], xph[:, :, 0:1]))
            else:
                pl.add("v", lambda e: e.tensor_copy(xp[:, :, 0:1], stash[:]))
            pl.add("v", lambda e, i=i: e.tensor_copy(xp[:, :, T + 1:T + 2], xph[:, :, i + 1:i + 2]))
            pl.drain("v")

            pl.wait("t", "dv")
            for j in range(16):
                ps, pv, idx = mq.group([((lambda p: p[:, :]),
                                         inW[:, k, 128 * j:128 * (j + 1)], nx2[:, k, :])
                                        for k in range(NCH)])
                if j < NCH:
                    pl.wait("v", "pe", pv)
                    pl.add("v", lambda e, ps=ps, j=j: e.tensor_scalar(
                        xp[:, j, 1:T + 1], ps[:, :], col("inb", j), None, ALU.add), inc="dv")
                    mq.done(idx, "dv", pl.val("dv"))
                else:
                    pl.wait("s", "pe", pv)
                    if j == 8 and i > 0:
                        pl.wait("s", "dv", ZC_DONE_PREV)
                    pl.add("s", lambda e, ps=ps, j=j: e.activation(
                        gate[:, j - 8, :], ps[:, :], AF.Identity, bias=col("inb", j)), inc="ac")
                    mq.done(idx, "ac", pl.val("ac"))

            def colB(name):
                i0 = PA_COLS[name]
                return cols[:, i0:i0 + 8].rearrange("p (c o) -> p c o", o=1).broadcast_to((128, NCH, T))

            pl.wait("g", "dv")
            pl.add("g", lambda e: e.tensor_tensor(mslt[:], xp[:, :, 0:T], colB("dw0"), ALU.mult))
            pl.add("g", lambda e: e.tensor_tensor(scr[:], xp[:, :, 1:T + 1], colB("dw1"), ALU.mult))
            pl.drain("g")
            pl.add("g", lambda e: e.tensor_tensor(mslt[:], mslt[:], scr[:], ALU.add))
            pl.drain("g")
            pl.add("g", lambda e: e.tensor_tensor(scr[:], xp[:, :, 2:T + 2], colB("dw2"), ALU.mult))
            pl.drain("g")
            pl.add("g", lambda e: e.tensor_tensor(mslt[:], mslt[:], scr[:], ALU.add))
            pl.drain("g")
            pl.add("g", lambda e: e.tensor_tensor(mslt[:], mslt[:], colB("dwb"), ALU.add))
            pl.drain("g")
            pl.add("g", lambda e: e.tensor_copy(stash[:], xp[:, :, T:T + 1]))
            pl.drain("g")
            pl.add("g", lambda e: e.tensor_copy(mbf[:], mslt[:]), inc="pl", n=1)
            M_DONE = pl.val("pl")
            M_DONE_PREV = M_DONE

            pl.wait("t", "pl", M_DONE)
            for j in range(NCH):
                ps, pv, idx = mq.group([((lambda p: p[:, :]),
                                         dtW[:, k, 128 * j:128 * (j + 1)], mbf[:, k, :])
                                        for k in range(NCH)])
                pl.wait("v", "pe", pv)
                pl.add("v", lambda e, ps=ps, j=j: e.tensor_scalar(
                    spx[:, j, :], ps[:, :], col("dtb", j), None, ALU.add), inc="dv")
                mq.done(idx, "dv", pl.val("dv"))
            pl.drain("v")
            pl.add("v", lambda e: e.tensor_single_scalar(scr[:], spx[:], -1.0, ALU.mult))
            pl.drain("v")
            pl.add("v", lambda e: e.tensor_tensor(scr[:], spx[:], scr[:], ALU.max),
                   inc="dv")
            pl.wait("s", "dv")
            pl.add("s", lambda e: e.activation(scr[:], scr[:], AF.Exp, scale=col("kM1")))
            pl.drain("s")
            pl.add("s", lambda e: e.activation(scr[:], scr[:], AF.Ln, bias=col("k1")), inc="ac")
            pl.drain("v")
            pl.add("v", lambda e: e.tensor_single_scalar(spx[:], spx[:], 0.0, ALU.max))
            pl.drain("v")
            pl.wait("v", "ac")
            pl.add("v", lambda e: e.tensor_tensor(spx[:], spx[:], scr[:], ALU.add), inc="dv")
            pl.wait("s", "dv")
            for c in range(NCH):
                pl.add("s", lambda e, c=c: e.activation(scr[:, c, :], spx[:, c, :], AF.Exp,
                                                        scale=acol[:, c:c + 1]),
                       inc=("ac" if c == NCH - 1 else None))
            DECAY_DONE = pl.val("ac")

            for j in range(NCH):
                ps, pv, idx = mq.group([((lambda p: p[:, :]),
                                         bW[:, k, 128 * j:128 * (j + 1)], mbf[:, k, :])
                                        for k in range(NCH)])
                pl.wait("s", "pe", pv)
                if j == 0:
                    pl.wait("s", "ac", DECAY_DONE)
                pl.add("s", lambda e, ps=ps, j=j: e.activation(btu[:, j, :], ps[:, :], AF.Tanh,
                                                               bias=col("bb", j)), inc="ac")
                mq.done(idx, "ac", pl.val("ac"))
            for j in range(NCH):
                ps, pv, idx = mq.group([((lambda p: p[:, :]),
                                         cW[:, k, 128 * j:128 * (j + 1)], mbf[:, k, :])
                                        for k in range(NCH)])
                pl.wait("s", "pe", pv)
                pl.add("s", lambda e, ps=ps, j=j: e.activation(ctt[:, j, :], ps[:, :], AF.Tanh,
                                                               bias=col("cb", j)), inc="ac")
                mq.done(idx, "ac", pl.val("ac"))
            pl.add("s", lambda e: e.activation(gate[:], gate[:], AF.Sigmoid), inc="ac")
            SIG_DONE = pl.val("ac")

            pl.wait("v", "ac", SIG_DONE)
            pl.add("v", lambda e: e.tensor_tensor(btu[:], spx[:], btu[:], ALU.mult))
            pl.drain("v")
            pl.add("v", lambda e: e.tensor_tensor(btu[:], btu[:], mslt[:], ALU.mult))
            pl.drain("v")

            for c in range(NCH):
                pl.add("v", lambda e, c=c: e.tensor_tensor_scan(
                    ps_sA[:, 0:T], scr[:, c, :], btu[:, c, :], state[:, c:c + 1],
                    ALU.mult, ALU.add))
                pl.drain("v")
                pl.add("v", lambda e, c=c: e.tensor_copy(state[:, c:c + 1], ps_sA[:, T - 1:T]))
                pl.add("v", lambda e, c=c: e.tensor_tensor_scan(
                    ps_sA[:, T:2 * T], scr[:, c, :], zeros[:], astate[:, c:c + 1],
                    ALU.mult, ALU.add))
                pl.drain("v")
                pl.add("v", lambda e, c=c: e.tensor_copy(astate[:, c:c + 1], ps_sA[:, 2 * T - 1:2 * T]))
                pl.add("v", lambda e, c=c: e.tensor_tensor(ycr[:, c, :], ctt[:, c, :],
                                                           ps_sA[:, 0:T], ALU.mult))
                pl.drain("v")
                pl.add("v", lambda e, c=c: e.scalar_tensor_tensor(
                    ycr[:, c, :], mslt[:, c, :], col("dskip", c), ycr[:, c, :],
                    ALU.mult, ALU.add))
                pl.add("v", lambda e, c=c: e.tensor_tensor(ctt[:, c, :], ctt[:, c, :],
                                                           ps_sA[:, T:2 * T], ALU.mult))
                pl.drain("v")

            snm, sval = spfree
            if snm:
                pl.wait("v", snm, sval)
            pl.add("v", lambda e: e.tensor_tensor(zbf[:], ycr[:], gate[:], ALU.mult))
            pl.drain("v")
            pl.add("v", lambda e: e.tensor_tensor(Cbf[:], ctt[:], gate[:], ALU.mult), inc="dv")
            ZC_DONE_PREV = pl.val("dv")
            pl.wait("y", "dv")
            pl.add("y", lambda e, i=i: e.dma_start(out=z_h[:, :, i * T:(i + 1) * T], in_=zbf[:]),
                   inc="sp", n=16)
            pl.add("y", lambda e, i=i: e.dma_start(out=C_h[:, :, i * T:(i + 1) * T], in_=Cbf[:]),
                   inc="sp", n=16)
            spfree = ("sp", pl.val("sp"))

        pl.wait("y", "dv")
        pl.add("y", lambda e: e.dma_start(out=PS_h[:, 0:8], in_=astate[:]), inc="sp", n=16)
        pl.add("y", lambda e: e.dma_start(out=PS_h[:, 8:16], in_=state[:]), inc="sp", n=16)
        pl.wait("y", "sp")

        pl.build()
    return nc



def fm(a):
    return np.ascontiguousarray(a.T.reshape(NCH, 128, -1).transpose(1, 0, 2))


def wchunks(w):
    K, J = w.shape
    return np.ascontiguousarray(w.reshape(K // 128, 128, J).transpose(1, 0, 2))


def colvec(v):
    return np.ascontiguousarray(v.reshape(NCH, 128).T)


def passA_inputs(inputs, TOK=4096, T=256):
    NT = TOK // T
    x = np.asarray(inputs["L0_attn_x"])
    cols = np.zeros((NCORE, 128, PA_NCOL), np.float32)
    for nm, src in [("ln1g", "L0_attn_ln1_g"), ("ln1b", "L0_attn_ln1_b"),
                    ("dwb", "L0_attn_dw_b"), ("dtb", "L0_attn_dt_b"),
                    ("bb", "L0_attn_b_b"), ("cb", "L0_attn_c_b"),
                    ("alog", "L0_attn_a_log"), ("dskip", "L0_attn_d_skip")]:
        cols[:, :, PA_COLS[nm]:PA_COLS[nm] + 8] = colvec(np.asarray(inputs[src]))[None]
    cols[:, :, PA_COLS["inb"]:PA_COLS["inb"] + 16] = np.asarray(
        inputs["L0_attn_in_b"]).reshape(16, 128).T[None]
    dw = np.asarray(inputs["L0_attn_dw_W"])
    for t in range(3):
        cols[:, :, PA_COLS[f"dw{t}"]:PA_COLS[f"dw{t}"] + 8] = colvec(dw[:, 0, t])[None]
    inW = wchunks(np.asarray(inputs["L0_attn_in_W"]))
    dtW = wchunks(np.asarray(inputs["L0_attn_dt_W"]))
    bW = wchunks(np.asarray(inputs["L0_attn_b_W"]))
    cW = wchunks(np.asarray(inputs["L0_attn_c_W"]))
    maps = []
    for core in range(NCORE):
        b, pos = divmod(core, NCORE // NB)
        a = pos * TOK
        xb = fm(x[b])
        xs = np.zeros((128, NCH, TOK + 2), np.float32)
        lo, hi = max(a - 1, 0), min(a + TOK + 1, SEQ)
        xs[:, :, lo - (a - 1):hi - (a - 1)] = xb[:, :, lo:hi]
        xbnd = np.zeros((128, NCH, NT + 1), np.float32)
        xbnd[:, :, 0] = xs[:, :, 0]
        for t in range(1, NT + 1):
            xbnd[:, :, t] = xs[:, :, t * T + 1]
        ccols = cols[core].copy()
        ccols[:, PA_COLS["maskL"]] = 0.0 if a == 0 else 1.0
        ccols[:, PA_COLS["maskR"]] = 0.0 if a + TOK >= SEQ else 1.0
        maps.append(dict(x=xs, xbnd=xbnd, inW=inW, dtW=dtW, bW=bW, cW=cW, cols=ccols))
    return maps


def _run(nc, maps):
    res = run_bass_kernel_spmd(nc, maps, core_ids=list(range(len(maps))))
    return res.results


def kernel(**inputs):
    TOK = SEQ // (NCORE // NB)
    TA = 256
    cores = [(k // (NCORE // NB), (k % (NCORE // NB)) * TOK) for k in range(NCORE)]
    x = np.asarray(inputs["L0_attn_x"])

    mapsA = passA_inputs_geom(inputs, cores, TOK, TA)
    resA = _run(build_mamba_passA(TOK, TA), mapsA)
    mapsB = passB_inputs_geom(inputs, cores, TOK, TA, resA)
    resB = _run(build_mamba_passB(TOK, TA), mapsB)
    x1 = [np.asarray(r["x1"]) for r in resB]

    def with_halo(tensors, h):
        outs = []
        for ci, (b, a) in enumerate(cores):
            xs = np.zeros((128, NCH, h + TOK), np.float32)
            xs[:, :, h:] = tensors[ci]
            if a > 0:
                xs[:, :, 0:h] = tensors[ci - 1][:, :, TOK - h:]
            outs.append(xs)
        return outs

    maps3 = lattn_inputs_geom(inputs, cores, TOK, 0, True, with_halo(x1, HALO))
    res3 = _run(build_lattn(TOK, True), maps3)
    xm1 = [np.asarray(r["xmid"]) for r in res3]
    maps4 = lffn_inputs_geom(inputs, cores, TOK, 0, False, with_halo(xm1, 1))
    res4 = _run(build_lffn(TOK, False, TA), maps4)
    z1 = [np.asarray(r["zout"]) for r in res4]
    maps5 = lattn_inputs_geom(inputs, cores, TOK, 1, False, with_halo(z1, HALO))
    res5 = _run(build_lattn(TOK, False), maps5)
    xm2 = [np.asarray(r["xmid"]) for r in res5]
    maps6 = lffn_inputs_geom(inputs, cores, TOK, 1, True, with_halo(xm2, 1), x1)
    res6 = _run(build_lffn(TOK, True, TA), maps6)

    out = np.empty((NB, SEQ, D), np.float32)
    for ci, (b, a) in enumerate(cores):
        zo = np.asarray(res6[ci]["zout"])
        out[b, a:a + TOK] = zo.transpose(1, 0, 2).reshape(D, TOK).T
    return out


def passA_inputs_geom(inputs, cores, TOK, T):
    NT = TOK // T
    x = np.asarray(inputs["L0_attn_x"])
    cols0 = np.zeros((128, PA_NCOL), np.float32)
    for nm, src in [("ln1g", "L0_attn_ln1_g"), ("ln1b", "L0_attn_ln1_b"),
                    ("dwb", "L0_attn_dw_b"), ("dtb", "L0_attn_dt_b"),
                    ("bb", "L0_attn_b_b"), ("cb", "L0_attn_c_b"),
                    ("alog", "L0_attn_a_log"), ("dskip", "L0_attn_d_skip")]:
        cols0[:, PA_COLS[nm]:PA_COLS[nm] + 8] = colvec(np.asarray(inputs[src]))
    cols0[:, PA_COLS["inb"]:PA_COLS["inb"] + 16] = np.asarray(
        inputs["L0_attn_in_b"]).reshape(16, 128).T
    dw = np.asarray(inputs["L0_attn_dw_W"])
    for t in range(3):
        cols0[:, PA_COLS[f"dw{t}"]:PA_COLS[f"dw{t}"] + 8] = colvec(dw[:, 0, t])
    inW = wchunks(np.asarray(inputs["L0_attn_in_W"]))
    dtW = wchunks(np.asarray(inputs["L0_attn_dt_W"]))
    bW = wchunks(np.asarray(inputs["L0_attn_b_W"]))
    cW = wchunks(np.asarray(inputs["L0_attn_c_W"]))
    xfms = [fm(x[b]) for b in range(x.shape[0])]
    maps = []
    for (b, a) in cores:
        xb = xfms[b]
        xs = np.zeros((128, NCH, TOK + 2), np.float32)
        lo, hi = max(a - 1, 0), min(a + TOK + 1, SEQ)
        xs[:, :, lo - (a - 1):hi - (a - 1)] = xb[:, :, lo:hi]
        xbnd = np.zeros((128, NCH, NT + 1), np.float32)
        xbnd[:, :, 0] = xs[:, :, 0]
        for t in range(1, NT + 1):
            xbnd[:, :, t] = xs[:, :, t * T + 1]
        cc = cols0.copy()
        cc[:, PA_COLS["kEPS"]] = EPS
        cc[:, PA_COLS["kM05"]] = -0.5
        cc[:, PA_COLS["kM1"]] = -1.0
        cc[:, PA_COLS["k1"]] = 1.0
        cc[:, PA_COLS["maskL"]] = 0.0 if a == 0 else 1.0
        cc[:, PA_COLS["maskR"]] = 0.0 if a + TOK >= SEQ else 1.0
        maps.append(dict(x=xs, xbnd=xbnd, inW=inW, dtW=dtW, bW=bW, cW=cW, cols=cc))
    return maps


PB_COLS = {}
_c = 0
for _nm, _w in [("ln2g", 8), ("ln2b", 8), ("outb", 8), ("f1b", 16), ("f2b", 8),
                ("mj", 8), ("omj", 8), ("kEPS", 1), ("kM05", 1)]:
    PB_COLS[_nm] = _c
    _c += _w
PB_NCOL = _c


def build_mamba_passB(TOK, T=256):
    NT = TOK // T
    nc = bass.Bass("TRN2", target_bir_lowering=False)
    x_h = nc.dram_tensor("x", [128, NCH, TOK], F32, kind="ExternalInput").ap()
    z_h = nc.dram_tensor("z", [128, NCH, TOK], BF16, kind="ExternalInput").ap()
    C_h = nc.dram_tensor("C", [128, NCH, TOK], BF16, kind="ExternalInput").ap()
    PSg_h = nc.dram_tensor("PSg", [128, 128], F32, kind="ExternalInput").ap()
    outW_h = nc.dram_tensor("outW", [128, NCH, D], F32, kind="ExternalInput").ap()
    f1W_h = nc.dram_tensor("f1W", [128, NCH, 2 * D], F32, kind="ExternalInput").ap()
    f2W_h = nc.dram_tensor("f2W", [128, 16, D], F32, kind="ExternalInput").ap()
    cols_h = nc.dram_tensor("cols", [128, PB_NCOL], F32, kind="ExternalInput").ap()
    x1_h = nc.dram_tensor("x1", [128, NCH, TOK], F32, kind="ExternalOutput").ap()

    with contextlib.ExitStack() as st:
        def sb(name, shape, dtype):
            return st.enter_context(nc.sbuf_tensor(name, shape, dtype))

        def pt(name, shape):
            return st.enter_context(nc.psum_tensor(name, shape, F32))

        outW = sb("outW_s", [128, NCH, D], BF16)
        f1W = sb("f1W_s", [128, NCH, 2 * D], BF16)
        f2W = sb("f2W_s", [128, 16, D], BF16)
        cols = sb("cols_s", [128, PB_NCOL], F32)
        PSg = sb("PSg_s", [128, 128], F32)
        sinit = sb("sinit", [128, NCH], F32)
        stmp = sb("stmp", [128, NCH], F32)
        ones_c = sb("ones_c", [128, 1], BF16)
        ones_r = sb("ones_r", [1, 128], BF16)
        xsl = [sb(f"xs{i}", [128, NCH, T], F32) for i in range(2)]
        zsl = [sb(f"zs{i}", [128, NCH, T], BF16) for i in range(2)]
        Csl = [sb(f"Cc{i}", [128, NCH, T], BF16) for i in range(2)]
        cbf = sb("cbf", [128, NCH, T], BF16)
        sqbf = sb("sqbf", [128, NCH, T], BF16)
        nxs = sb("nxs", [128, NCH, T], F32)
        zf = sb("zf", [128, NCH, T], F32)
        zfb = sb("zfb", [128, NCH, T], BF16)
        x2 = sb("x2", [128, NCH, T], F32)
        hb = sb("hb", [128, NCH, T], BF16)
        h1 = sb("h1", [128, 16, T], BF16)
        x1t = sb("x1t", [128, NCH, T], F32)
        rowa = sb("rowa", [1, 4 * T], F32)
        rowhl = sb("rowhl", [1, 2 * T], BF16)

        ps_stat = pt("ps_stat", [1, 2 * T])
        ps_ab = pt("ps_ab", [128, 2 * T])
        ps_mm = [pt(f"ps_mm{i}", [128, T]) for i in range(2)]

        pl = Plan(nc, st)

        def col(name, c=0):
            i = PB_COLS[name] + c
            return cols[:, i:i + 1]

        kEPS = cols[0:1, PB_COLS["kEPS"]:PB_COLS["kEPS"] + 1]
        kM05 = cols[0:1, PB_COLS["kM05"]:PB_COLS["kM05"] + 1]

        pl.add("g", lambda e: e.dma_start(out=outW[:], in_=outW_h[:]), inc="wld", n=16)
        pl.add("g", lambda e: e.dma_start(out=f1W[:], in_=f1W_h[:]), inc="wld", n=16)
        pl.add("g", lambda e: e.dma_start(out=f2W[:], in_=f2W_h[:]), inc="wld", n=16)
        pl.add("g", lambda e: e.dma_start(out=cols[:], in_=cols_h[:]), inc="wld", n=16)
        pl.add("g", lambda e: e.dma_start(out=PSg[:], in_=PSg_h[:]), inc="wld", n=16)
        WLD = pl.val("wld")
        for eng in ("v", "s", "t"):
            pl.wait(eng, "wld", WLD)
        pl.add("v", lambda e: e.memset(ones_c[:], 1.0))
        pl.add("v", lambda e: e.memset(ones_r[:], 1.0))
        pl.add("v", lambda e: e.memset(sinit[:], 0.0))
        pl.drain("v")
        for j in range(NCORE):
            pl.add("v", lambda e, j=j: e.tensor_scalar(
                stmp[:], PSg[:, 16 * j:16 * j + 8], col("mj", j), col("omj", j),
                ALU.mult, ALU.add))
            pl.drain("v")
            pl.add("v", lambda e, j=j: e.tensor_tensor(sinit[:], sinit[:], stmp[:], ALU.mult))
            pl.drain("v")
            pl.add("v", lambda e, j=j: e.tensor_scalar(
                stmp[:], PSg[:, 16 * j + 8:16 * j + 16], col("mj", j), None, ALU.mult))
            pl.drain("v")
            pl.add("v", lambda e, j=j: e.tensor_tensor(sinit[:], sinit[:], stmp[:], ALU.add))
            pl.drain("v")

        ldfree = {}

        def load(i, sl, dst, src):
            key = f"ld{dst._name if hasattr(dst, '_name') else id(dst)}{sl}"
            key = f"ld{id(dst) % 997}_{sl}"
            snm, sval = ldfree.get(key, ("", 0))
            if snm:
                pl.wait("y", snm, sval)
            pl.add("y", lambda e, i=i, dst=dst, src=src: e.dma_start(
                out=dst[:], in_=src[:, :, i * T:(i + 1) * T]), inc=key, n=16)
            return key

        kx = load(0, 0, xsl[0], x_h)
        kz = load(0, 0, zsl[0], z_h)
        kC = load(0, 0, Csl[0], C_h)
        mq = MMQ(pl, ps_mm)
        stfree = ("", 0)

        for i in range(NT):
            sl = i % 2
            xt, zt, Ct = xsl[sl], zsl[sl], Csl[sl]
            for key in (f"ld{id(xt) % 997}_{sl}", f"ld{id(zt) % 997}_{sl}", f"ld{id(Ct) % 997}_{sl}"):
                pl.wait("v", key)
                pl.wait("s", key)
            pl.add("v", lambda e, xt=xt: e.tensor_copy(cbf[:], xt[:]), inc="dv")
            for c in range(NCH):
                pl.add("s", lambda e, xt=xt, c=c: e.activation(sqbf[:, c, :], xt[:, c, :], AF.Square),
                       inc=("ac" if c == NCH - 1 else None))
            pl.wait("t", "dv")
            pl.wait("t", "ac")
            for c in range(NCH):
                pl.add("t", lambda e, c=c: e.matmul(ps_stat[0:1, 0:T], ones_c[:], cbf[:, c, :],
                                                    start=(c == 0), stop=(c == NCH - 1)))
            for c in range(NCH):
                pl.add("t", lambda e, c=c: e.matmul(ps_stat[0:1, T:2 * T], ones_c[:], sqbf[:, c, :],
                                                    start=(c == 0), stop=(c == NCH - 1)),
                       inc=("pe" if c == NCH - 1 else None))
            pl.wait("v", "pe")
            alpha, beta = emit_ln_rows(pl, ps_stat, rowa, T, T, double=False, tag="r",
                                       kEPS=kEPS, kM05=kM05)
            emit_ab_bcast(pl, rowhl, alpha, beta, ones_r, ps_ab, T, T, tag="b")
            for c in range(NCH):
                pl.add("v", lambda e, xt=xt, c=c: e.tensor_tensor(
                    nxs[:, c, :], xt[:, c, :], ps_ab[:, 0:T], ALU.mult))
            pl.drain("v")
            for c in range(NCH):
                pl.add("v", lambda e, c=c: e.tensor_tensor(
                    nxs[:, c, :], nxs[:, c, :], ps_ab[:, T:2 * T], ALU.subtract),
                    inc=("dv" if c == NCH - 1 else None))
            sB = sinit[:].rearrange("p (c o) -> p c o", o=1).broadcast_to((128, NCH, T))
            pl.add("v", lambda e, Ct=Ct: e.tensor_tensor(zf[:], Ct[:], sB, ALU.mult))
            pl.drain("v")
            pl.add("v", lambda e, zt=zt: e.tensor_tensor(zf[:], zf[:], zt[:], ALU.add))
            pl.drain("v")
            pl.add("v", lambda e: e.tensor_copy(zfb[:], zf[:]), inc="dv")
            ZF_DONE = pl.val("dv")
            if i + 1 < NT:
                load(i + 1, (i + 1) % 2, xsl[(i + 1) % 2], x_h)
                load(i + 1, (i + 1) % 2, zsl[(i + 1) % 2], z_h)
                load(i + 1, (i + 1) % 2, Csl[(i + 1) % 2], C_h)
            pl.wait("t", "dv", ZF_DONE)
            for j in range(NCH):
                ps, pv, idx = mq.group([((lambda p: p[:, :]),
                                         outW[:, k, 128 * j:128 * (j + 1)], zfb[:, k, :])
                                        for k in range(NCH)])
                pl.wait("v", "pe", pv)
                pl.add("v", lambda e, ps=ps, j=j: e.scalar_tensor_tensor(
                    x2[:, j, :], ps[:, :], col("outb", j), nxs[:, j, :], ALU.add, ALU.add),
                    inc="dv")
                mq.done(idx, "dv", pl.val("dv"))
            X2D = pl.val("dv")
            pl.drain("v")
            pl.add("v", lambda e: e.tensor_copy(cbf[:], x2[:]), inc="dv")
            pl.wait("s", "dv", X2D)
            for c in range(NCH):
                pl.add("s", lambda e, c=c: e.activation(sqbf[:, c, :], x2[:, c, :], AF.Square),
                       inc=("ac" if c == NCH - 1 else None))
            pl.wait("t", "dv")
            pl.wait("t", "ac")
            for c in range(NCH):
                pl.add("t", lambda e, c=c: e.matmul(ps_stat[0:1, 0:T], ones_c[:], cbf[:, c, :],
                                                    start=(c == 0), stop=(c == NCH - 1)))
            for c in range(NCH):
                pl.add("t", lambda e, c=c: e.matmul(ps_stat[0:1, T:2 * T], ones_c[:], sqbf[:, c, :],
                                                    start=(c == 0), stop=(c == NCH - 1)),
                       inc=("pe" if c == NCH - 1 else None))
            pl.wait("v", "pe")
            alpha, beta = emit_ln_rows(pl, ps_stat, rowa, T, T, double=False, tag="r",
                                       kEPS=kEPS, kM05=kM05)
            emit_ab_bcast(pl, rowhl, alpha, beta, ones_r, ps_ab, T, T, tag="b")
            for c in range(NCH):
                pl.add("v", lambda e, c=c: e.tensor_tensor(
                    zf[:, c, :], x2[:, c, :], ps_ab[:, 0:T], ALU.mult))
            pl.drain("v")
            for c in range(NCH):
                pl.add("v", lambda e, c=c: e.tensor_tensor(
                    zf[:, c, :], zf[:, c, :], ps_ab[:, T:2 * T], ALU.subtract))
            pl.drain("v")
            for c in range(NCH):
                pl.add("v", lambda e, c=c: e.tensor_scalar(
                    hb[:, c, :], zf[:, c, :], col("ln2g", c), col("ln2b", c),
                    ALU.mult, ALU.add),
                    inc=("dv" if c == NCH - 1 else None))
            HB_DONE = pl.val("dv")
            pl.wait("t", "dv", HB_DONE)
            for j in range(16):
                ps, pv, idx = mq.group([((lambda p: p[:, :]),
                                         f1W[:, k, 128 * j:128 * (j + 1)], hb[:, k, :])
                                        for k in range(NCH)])
                pl.wait("s", "pe", pv)
                pl.add("s", lambda e, ps=ps, j=j: e.activation(
                    h1[:, j, :], ps[:, :], AF.Gelu, bias=col("f1b", j)), inc="ac")
                mq.done(idx, "ac", pl.val("ac"))
            pl.wait("t", "ac")
            for j in range(NCH):
                ps, pv, idx = mq.group([((lambda p: p[:, :]),
                                         f2W[:, k, 128 * j:128 * (j + 1)], h1[:, k, :])
                                        for k in range(16)])
                pl.wait("v", "pe", pv)
                pl.add("v", lambda e, ps=ps, j=j: e.scalar_tensor_tensor(
                    x1t[:, j, :], ps[:, :], col("f2b", j), x2[:, j, :], ALU.add, ALU.add))
                pl.drain("v")
                pl.add("v", lambda e, xt=xt, j=j: e.tensor_tensor(
                    x1t[:, j, :], x1t[:, j, :], xt[:, j, :], ALU.add), inc="dv")
                mq.done(idx, "dv", pl.val("dv"))
            X1_DONE = pl.val("dv")
            snm, sval = stfree
            if snm:
                pl.wait("y", snm, sval)
            pl.wait("y", "dv", X1_DONE)
            pl.add("y", lambda e, i=i: e.dma_start(out=x1_h[:, :, i * T:(i + 1) * T], in_=x1t[:]),
                   inc="st", n=16)
            stfree = ("st", pl.val("st"))
            stv = pl.val("st")
            pl.wait("v", "st", stv)

        pl.wait("y", "st")
        pl.build()
    return nc


def passB_inputs_geom(inputs, cores, TOK, T, passA_results):
    x = np.asarray(inputs["L0_attn_x"])
    cols0 = np.zeros((128, PB_NCOL), np.float32)
    for nm, src in [("ln2g", "L0_attn_ln2_g"), ("ln2b", "L0_attn_ln2_b"),
                    ("outb", "L0_attn_out_b"), ("f2b", "L0_attn_f2_b")]:
        cols0[:, PB_COLS[nm]:PB_COLS[nm] + 8] = colvec(np.asarray(inputs[src]))
    cols0[:, PB_COLS["f1b"]:PB_COLS["f1b"] + 16] = np.asarray(
        inputs["L0_attn_f1_b"]).reshape(16, 128).T
    cols0[:, PB_COLS["kEPS"]] = EPS
    cols0[:, PB_COLS["kM05"]] = -0.5
    outW = wchunks(np.asarray(inputs["L0_attn_out_W"]))
    f1W = wchunks(np.asarray(inputs["L0_attn_f1_W"]))
    f2W = wchunks(np.asarray(inputs["L0_attn_f2_W"]))
    PSg = np.zeros((128, 128), np.float32)
    for j, r in enumerate(passA_results):
        PSg[:, 16 * j:16 * j + 16] = np.asarray(r["PS"])
    xfms = [fm(x[b]) for b in range(x.shape[0])]
    maps = []
    for ci, (b, a) in enumerate(cores):
        cc = cols0.copy()
        for j, (bj, aj) in enumerate(cores):
            m = 1.0 if (bj == b and aj < a) else 0.0
            cc[:, PB_COLS["mj"] + j] = m
            cc[:, PB_COLS["omj"] + j] = 1.0 - m
        maps.append(dict(x=np.ascontiguousarray(xfms[b][:, :, a:a + TOK]),
                         z=np.asarray(passA_results[ci]["z"]),
                         C=np.asarray(passA_results[ci]["C"]),
                         PSg=PSg, outW=outW, f1W=f1W, f2W=f2W, cols=cc))
    return maps


LA_NCOL = 3
HALO = 513
WIN = 512


def build_lattn(TOK, pre_ln):
    NW = TOK // WIN
    T = WIN
    nc = bass.Bass("TRN2", target_bir_lowering=False)
    x_h = nc.dram_tensor("x", [128, NCH, HALO + TOK], F32, kind="ExternalInput").ap()
    wqkv_h = nc.dram_tensor("wqkv", [128, NCH, 1536], F32, kind="ExternalInput").ap()
    wo_h = nc.dram_tensor("wo", [128, 4, D], F32, kind="ExternalInput").ap()
    bq_h = nc.dram_tensor("bq", [2, 128], F32, kind="ExternalInput").ap()
    bk_h = nc.dram_tensor("bk", [2, 128], F32, kind="ExternalInput").ap()
    csq_h = nc.dram_tensor("csq", [128, WIN + TOK], F32, kind="ExternalInput").ap()
    ssq_h = nc.dram_tensor("ssq", [128, WIN + TOK], F32, kind="ExternalInput").ap()
    csk_h = nc.dram_tensor("csk", [128, WIN + TOK], F32, kind="ExternalInput").ap()
    ssk_h = nc.dram_tensor("ssk", [128, WIN + TOK], F32, kind="ExternalInput").ap()
    msk_h = nc.dram_tensor("msk", [128, 5, 640], BF16, kind="ExternalInput").ap()
    eye_h = nc.dram_tensor("eye", [128, 128], F32, kind="ExternalInput").ap()
    eye2_h = nc.dram_tensor("eye2", [128, 64], F32, kind="ExternalInput").ap()
    cols_h = nc.dram_tensor("cols", [128, LA_NCOL], F32, kind="ExternalInput").ap()
    xm_h = nc.dram_tensor("xmid", [128, NCH, TOK], F32, kind="ExternalOutput").ap()

    with contextlib.ExitStack() as st:
        def sb(name, shape, dtype):
            return st.enter_context(nc.sbuf_tensor(name, shape, dtype))

        def pt(name, shape, dtype=F32):
            return st.enter_context(nc.psum_tensor(name, shape, dtype))

        wqkv = sb("wqkv_s", [128, NCH, 1536], BF16)
        wo = sb("wo_s", [128, 4, D], BF16)
        bq = sb("bq_s", [2, 128], BF16)
        bk = sb("bk_s", [2, 128], BF16)
        bones = sb("bones", [128, 2], BF16)
        ones_c = sb("ones_c", [128, 1], BF16)
        ones_r = sb("ones_r", [1, 128], BF16)
        eye = sb("eye_s", [128, 128], BF16)
        eyef = sb("eyef_s", [128, 128], F32)
        eye2 = sb("eye2_s", [128, 64], BF16)
        eye2f = sb("eye2f_s", [128, 64], F32)
        cols = sb("cols_s", [128, LA_NCOL], F32)
        msk = sb("msk_s", [128, 5, 640], BF16)
        xsl = [sb(f"xs{i}", [128, NCH, T], F32) for i in range(2)]
        zb = sb("zbuf", [128, NCH, T + 1], BF16)
        cbf = sb("cbf", [128, NCH, T], BF16)
        sqbf = sb("sqbf2", [128, NCH, T], BF16)
        zf = sb("zft", [128, NCH, T], F32)
        qf = sb("qft", [128, 4, T], F32)
        kf = sb("kft", [128, 4, T], F32)
        qqp = sb("qqp", [128, 4, T], F32)
        q2b = sb("q2b", [128, 4, T], BF16)
        qrotb = sb("qrotb", [128, 4, T], BF16)
        krotb = [sb(f"krotb{i}", [128, 4, T], BF16) for i in range(2)]
        vbf = sb("vbft", [128, 4, T], BF16)
        vtm = [sb(f"vtm{i}", [128, 4, 4, 128], BF16) for i in range(2)]
        nrm = sb("nrm", [2, 4, T], F32)
        nrmb = sb("nrmb", [2, 4, T], BF16)
        tabs = sb("tabs", [128, 4, T], F32)
        scs = sb("scs", [128, 640], F32)
        pbf = sb("pbft", [128, 640], BF16)
        pTs = sb("pTs", [128, 5, 128], BF16)
        obf = sb("obft", [128, 4, T], BF16)
        mxc = sb("mxc", [128, 2], F32)
        rsc = sb("rsc", [128, 2], F32)
        rowa = sb("rowa", [1, 4 * T], F32)
        rowhl = sb("rowhl", [1, 2 * T], BF16)
        xc0 = sb("xc0", [128, NCH, 2], F32)
        c0b = sb("c0b", [128, NCH, 1], BF16)

        ps_stat = pt("ps_stat", [1, T])
        ps_ab = pt("ps_ab", [128, 2 * T])
        ps_mm = [pt(f"ps_mm{i}", [128, T]) for i in range(2)]
        ps_nrm = pt("ps_nrm", [2, T])
        ps_pT = pt("ps_pT", [128, 640], BF16)
        ps_o = pt("ps_o", [128, 2, 128])

        pl = Plan(nc, st)
        kEPS = cols[0:1, 0:1]
        kM05 = cols[0:1, 1:2]
        kEPSN = cols[:, 2:3]

        for (dst, srch) in [(wqkv, wqkv_h), (wo, wo_h), (bq, bq_h), (bk, bk_h),
                            (eyef, eye_h), (eye2f, eye2_h), (msk, msk_h), (cols, cols_h)]:
            pl.add("g", lambda e, dst=dst, srch=srch: e.dma_start(out=dst[:], in_=srch[:]),
                   inc="wld", n=16)
        WLD = pl.val("wld")
        for eng in ("v", "s", "t"):
            pl.wait(eng, "wld", WLD)
        pl.add("v", lambda e: e.memset(ones_c[:], 1.0))
        pl.add("v", lambda e: e.memset(ones_r[:], 1.0))
        pl.add("v", lambda e: e.memset(bones[:], 0.0))
        pl.drain("v")
        pl.add("v", lambda e: e.memset(bones[0:64, 0:1], 1.0))
        pl.add("v", lambda e: e.memset(bones[64:128, 1:2], 1.0))
        pl.drain("v")
        pl.add("v", lambda e: e.tensor_copy(eye[:], eyef[:]))
        pl.drain("v")
        pl.add("v", lambda e: e.tensor_copy(eye2[:], eye2f[:]), inc="dv")

        mq = MMQ(pl, ps_mm)
        stfree = ("", 0)
        xfree = [("", 0), ("", 0)]

        def load_x(w):
            sl = w % 2
            snm, sval = xfree[sl]
            if snm:
                pl.wait("y", snm, sval)
            pl.add("y", lambda e, w=w, sl=sl: e.dma_start(
                out=xsl[sl][:], in_=x_h[:, :, 1 + w * T:1 + (w + 1) * T]),
                inc=f"xld{sl}", n=16)

        load_x(0)
        pl.add("y", lambda e: e.dma_start(out=xc0[:], in_=x_h[:, :, 0:2]), inc="c0ld", n=16)

        pl.wait("v", "c0ld")
        if pre_ln:
            pl.wait("s", "c0ld")
            pl.add("v", lambda e: e.tensor_copy(c0b[:], xc0[:, :, 0:1]), inc="dv")
            pl.add("s", lambda e: e.activation(sqbf[:, :, 0:1], xc0[:, :, 0:1], AF.Square), inc="ac")
            pl.wait("t", "dv")
            pl.wait("t", "ac")
            for c in range(NCH):
                pl.add("t", lambda e, c=c: e.matmul(ps_stat[0:1, 0:1], ones_c[:], c0b[:, c, :],
                                                    start=(c == 0), stop=(c == NCH - 1)))
            for c in range(NCH):
                pl.add("t", lambda e, c=c: e.matmul(ps_stat[0:1, 1:2], ones_c[:], sqbf[:, c, 0:1],
                                                    start=(c == 0), stop=(c == NCH - 1)),
                       inc=("pe" if c == NCH - 1 else None))
            pl.wait("v", "pe")
            pl.add("v", lambda e: e.tensor_single_scalar(rowa[0:1, 0:1], ps_stat[0:1, 0:1], 1.0 / D, ALU.mult))
            pl.add("v", lambda e: e.tensor_single_scalar(rowa[0:1, 1:2], ps_stat[0:1, 1:2], 1.0 / D, ALU.mult))
            pl.drain("v")
            pl.add("v", lambda e: e.tensor_tensor(rowa[0:1, 2:3], rowa[0:1, 0:1], rowa[0:1, 0:1], ALU.mult))
            pl.drain("v")
            pl.add("v", lambda e: e.tensor_tensor(rowa[0:1, 2:3], rowa[0:1, 1:2], rowa[0:1, 2:3], ALU.subtract), inc="rv")
            pl.wait("s", "rv")
            pl.add("s", lambda e: e.activation(rowa[0:1, 3:4], rowa[0:1, 2:3], AF.Ln, bias=kEPS))
            pl.drain("s")
            pl.add("s", lambda e: e.activation(rowa[0:1, 3:4], rowa[0:1, 3:4], AF.Exp, scale=kM05), inc="rs")
            pl.wait("v", "rs")
            pl.add("v", lambda e: e.tensor_tensor(rowa[0:1, 1:2], rowa[0:1, 0:1], rowa[0:1, 3:4], ALU.mult))
            pl.drain("v")
            emit_ab_bcast(pl, rowhl, rowa[0:1, 3:4], rowa[0:1, 1:2], ones_r, ps_ab, 1, T, tag="b")
            pl.add("v", lambda e: e.tensor_tensor(xc0[:, :, 0:1], xc0[:, :, 0:1],
                                                  ps_ab[:, 0:1].broadcast_to((128, NCH, 1)), ALU.mult))
            pl.drain("v")
            pl.add("v", lambda e: e.tensor_tensor(xc0[:, :, 0:1], xc0[:, :, 0:1],
                                                  ps_ab[:, T:T + 1].broadcast_to((128, NCH, 1)), ALU.subtract))
            pl.drain("v")
        pl.add("v", lambda e: e.tensor_copy(zb[:, :, 0:1], xc0[:, :, 0:1]), inc="dv")

        for w in range(NW + 1):
            sl = w % 2
            xt = xsl[sl]
            kcur, kprev = krotb[w % 2], krotb[(w + 1) % 2]
            vcur, vprev = vtm[w % 2], vtm[(w + 1) % 2]
            pl.wait("v", f"xld{sl}")
            pl.wait("s", f"xld{sl}")
            load_tabs = [(0, csq_h), (1, ssq_h), (2, csk_h), (3, ssk_h)]
            pl.wait("y", "dv", pl.val("dv"))
            for (ti, th) in load_tabs:
                pl.add("y", lambda e, ti=ti, th=th, w=w: e.dma_start(
                    out=tabs[:, ti, :], in_=th[:, w * T:(w + 1) * T]), inc="tld", n=16)
            if pre_ln:
                pl.drain("v")
                pl.add("v", lambda e, xt=xt: e.tensor_copy(cbf[:], xt[:]), inc="dv")
                pl.drain("s")
                for c in range(NCH):
                    pl.add("s", lambda e, xt=xt, c=c: e.activation(sqbf[:, c, :], xt[:, c, :], AF.Square),
                           inc=("ac" if c == NCH - 1 else None))
                pl.wait("t", "dv")
                pl.wait("t", "ac")
                for c in range(NCH):
                    pl.add("t", lambda e, c=c: e.matmul(ps_stat[0:1, 0:T], ones_c[:], cbf[:, c, :],
                                                        start=(c == 0), stop=(c == NCH - 1)),
                           inc=("pe" if c == NCH - 1 else None))
                pl.wait("v", "pe")
                pl.add("v", lambda e: e.tensor_single_scalar(rowa[0:1, 0:T], ps_stat[0:1, 0:T],
                                                             1.0 / D, ALU.mult), inc="dv")
                pl.wait("t", "dv")
                for c in range(NCH):
                    pl.add("t", lambda e, c=c: e.matmul(ps_stat[0:1, 0:T], ones_c[:], sqbf[:, c, :],
                                                        start=(c == 0), stop=(c == NCH - 1)),
                           inc=("pe" if c == NCH - 1 else None))
                pl.wait("v", "pe")
                pl.add("v", lambda e: e.tensor_single_scalar(rowa[0:1, T:2 * T], ps_stat[0:1, 0:T],
                                                             1.0 / D, ALU.mult))
                pl.drain("v")
                pl.add("v", lambda e: e.tensor_tensor(rowa[0:1, 2 * T:3 * T], rowa[0:1, 0:T],
                                                      rowa[0:1, 0:T], ALU.mult))
                pl.drain("v")
                pl.add("v", lambda e: e.tensor_tensor(rowa[0:1, T:2 * T], rowa[0:1, T:2 * T],
                                                      rowa[0:1, 2 * T:3 * T], ALU.subtract), inc="rv")
                pl.wait("s", "rv")
                pl.add("s", lambda e: e.activation(rowa[0:1, 2 * T:3 * T], rowa[0:1, T:2 * T],
                                                   AF.Ln, bias=kEPS))
                pl.drain("s")
                pl.add("s", lambda e: e.activation(rowa[0:1, 2 * T:3 * T], rowa[0:1, 2 * T:3 * T],
                                                   AF.Exp, scale=kM05), inc="rs")
                pl.wait("v", "rs")
                pl.add("v", lambda e: e.tensor_tensor(rowa[0:1, 3 * T:4 * T], rowa[0:1, 0:T],
                                                      rowa[0:1, 2 * T:3 * T], ALU.mult), inc="rv")
                emit_ab_bcast(pl, rowhl, rowa[0:1, 2 * T:3 * T], rowa[0:1, 3 * T:4 * T],
                              ones_r, ps_ab, T, T, tag="b")
                for c in range(NCH):
                    pl.add("v", lambda e, xt=xt, c=c: e.tensor_tensor(
                        zf[:, c, :], xt[:, c, :], ps_ab[:, 0:T], ALU.mult))
                pl.drain("v")
                for c in range(NCH):
                    pl.add("v", lambda e, c=c: e.tensor_tensor(
                        zf[:, c, :], zf[:, c, :], ps_ab[:, T:2 * T], ALU.subtract),
                        inc=("dv" if c == NCH - 1 else None))
            else:
                pl.drain("v")
                pl.add("v", lambda e, xt=xt: e.tensor_copy(zf[:], xt[:]), inc="dv")
            ZRD = pl.val("dv")
            xfree[sl] = ("dv", ZRD)
            if w + 1 < NW + 1:
                load_x(w + 1)
            pl.drain("v")
            if w > 0:
                pl.add("v", lambda e: e.tensor_copy(zb[:, :, 0:1], zb[:, :, T:T + 1]))
                pl.drain("v")
            pl.add("v", lambda e: e.tensor_copy(zb[:, :, 1:T + 1], zf[:]), inc="dv")
            ZBD = pl.val("dv")

            pl.wait("t", "dv", ZBD)
            for j in range(12):
                ps, pv, idx = mq.group([((lambda p: p[:, :]), wqkv[:, k, 128 * j:128 * (j + 1)],
                                         (zb[:, k, 1:T + 1] if k < 4 else zb[:, k, 0:T]))
                                        for k in range(NCH)])
                pl.wait("v", "pe", pv)
                if j < 4:
                    pl.add("v", lambda e, ps=ps, j=j: e.tensor_copy(qf[:, j, :], ps[:, :]), inc="dv")
                elif j < 8:
                    pl.add("v", lambda e, ps=ps, j=j: e.tensor_copy(kf[:, j - 4, :], ps[:, :]), inc="dv")
                else:
                    pl.add("v", lambda e, ps=ps, j=j: e.tensor_copy(vbf[:, j - 8, :], ps[:, :]), inc="dv")
                mq.done(idx, "dv", pl.val("dv"))
            QKVD = pl.val("dv")

            import os as _os
            _stage = _os.environ.get("LATTN_STAGE", "full")
            pl.wait("t", "dv", QKVD)
            for jb in (range(4) if _stage not in ("z",) else []):
                for c in range(4):
                    pl.add("t", lambda e, jb=jb, c=c: e.transpose(
                        ps_pT[:, 128 * c:128 * (c + 1)],
                        vbf[:, c, 128 * jb:128 * (jb + 1)],
                        eye[:]),
                        inc=("pe" if c == 3 else None))
                pv = pl.val("pe")
                pl.wait("v", "pe", pv)
                pl.add("v", lambda e, jb=jb, vcur=vcur: e.tensor_copy(
                    vcur[:, jb, :, :].rearrange("p h d1 -> p (h d1)"), ps_pT[:, 0:512]),
                    inc="dv")
                pl.wait("t", "dv")
            VTD = pl.val("dv")

            def norm_rot(src, perm, rotb, bsc, cs_i, ss_i, pmsem):
                pl.wait("s", "dv", pl.val("dv"))
                pl.drain("s")
                pl.add("s", lambda e, src=src: e.activation(q2b[:], src[:], AF.Square), inc="ac")
                SQ = pl.val("ac")
                pl.wait("t", "ac", SQ)
                for c in range(4):
                    pl.add("t", lambda e, c=c: e.matmul(ps_nrm[:, :], bones[:], q2b[:, c, :],
                                                        start=True, stop=True), inc="pe")
                    pv = pl.val("pe")
                    pl.wait("v", "pe", pv)
                    pl.drain("v")
                    pl.add("v", lambda e, c=c: e.tensor_copy(nrm[:, c, :], ps_nrm[:, :]), inc="dv")
                    pl.wait("t", "dv")
                pl.drain("v")
                pl.add("v", lambda e: e.tensor_scalar(nrm[:], nrm[:], kEPSN[0:2, :], None, ALU.max),
                       inc="rv")
                pl.wait("s", "rv")
                pl.add("s", lambda e: e.activation(nrm[:], nrm[:], AF.Ln))
                pl.drain("s")
                pl.add("s", lambda e: e.activation(nrm[:], nrm[:], AF.Exp, scale=cols[0:2, 1:2]), inc="rs")
                pl.wait("v", "rs")
                pl.add("v", lambda e: e.tensor_copy(nrmb[:], nrm[:]), inc="dv")
                NB_ = pl.val("dv")
                pl.wait("t", "dv", NB_)
                for c in range(4):
                    ps, pv, idx = mq.group([((lambda p: p[:, :]), bsc[:], nrmb[0:2, c, :])])
                    pl.wait("v", "pe", pv)
                    pl.drain("v")
                    pl.add("v", lambda e, ps=ps, src=src, c=c: e.tensor_tensor(
                        src[:, c, :], src[:, c, :], ps[:, :], ALU.mult), inc="dv")
                    mq.done(idx, "dv", pl.val("dv"))
                pl.wait("y", "dv", pl.val("dv"))
                for (od, sd) in [(0, 32), (32, 0), (64, 96), (96, 64)]:
                    pl.add("y", lambda e, od=od, sd=sd, src=src, perm=perm: e.dma_start(
                        out=perm[od:od + 32, :, :], in_=src[sd:sd + 32, :, :]), inc=pmsem, n=16)
                csB = tabs[:, cs_i, :].rearrange("p (o t2) -> p o t2", o=1).broadcast_to((128, 4, T))
                ssB = tabs[:, ss_i, :].rearrange("p (o t2) -> p o t2", o=1).broadcast_to((128, 4, T))
                pl.wait("v", "tld", pl.val("tld"))
                pl.wait("v", pmsem, pl.val(pmsem))
                pl.drain("v")
                pl.add("v", lambda e, src=src: e.tensor_tensor(src[:], src[:], csB, ALU.mult))
                pl.drain("v")
                pl.add("v", lambda e, perm=perm: e.tensor_tensor(perm[:], perm[:], ssB, ALU.mult))
                pl.drain("v")
                pl.add("v", lambda e, src=src, perm=perm, rotb=rotb: e.tensor_tensor(
                    rotb[:], src[:], perm[:], ALU.add), inc="dv")

            if _stage not in ("z", "vt"):
                norm_rot(kf, qqp, kcur, bk, 2, 3, "pmk")
                if w > 0:
                    norm_rot(qf, qqp, qrotb, bq, 0, 1, "pmq")
            RTD = pl.val("dv")

            if w > 0 and _stage in ("attn", "full"):
                pl.wait("t", "dv", RTD)
                oslot = 0
                for s in range(4):
                    for h in range(8):
                        hb64 = 64 * (h % 2)
                        hc = h // 2
                        lq = qrotb[hb64:hb64 + 64, hc, 128 * s:128 * s + 128]
                        segs = []
                        if T - 128 * s > 0:
                            segs.append((kprev, 128 * s, T, 0))
                        if s > 0:
                            segs.append((kcur, 0, 128 * s, T - 128 * s))
                        segs.append((kcur, 128 * s, 128 * s + 128, None))
                        nseg = len(segs)
                        for q_, (ksl, j0, j1, off) in enumerate(segs):
                            if off is None:
                                out = ps_ab[:, T:T + 128]
                            else:
                                out = ps_ab[:, off:off + (j1 - j0)]
                            pl.add("t", lambda e, lq=lq, ksl=ksl, j0=j0, j1=j1, out=out, hb64=hb64, hc=hc:
                                   e.matmul(out, lq, ksl[hb64:hb64 + 64, hc, j0:j1],
                                            start=True, stop=True),
                                   inc=("pe" if q_ == nseg - 1 else None))
                        pv = pl.val("pe")
                        pl.wait("v", "pe", pv)
                        mi = (1 + s) if w == 1 else 0
                        pl.drain("v")
                        pl.add("v", lambda e, mi=mi: e.tensor_tensor(
                            scs[:, 0:512], ps_ab[:, 0:T], msk[:, mi, 0:512], ALU.add))
                        pl.add("v", lambda e, mi=mi: e.tensor_tensor(
                            scs[:, 512:640], ps_ab[:, T:T + 128], msk[:, mi, 512:640], ALU.add))
                        pl.drain("v")
                        pl.add("v", lambda e: e.tensor_reduce(
                            mxc[:, 0:1], scs[:], mybir.AxisListType.X, ALU.max, negate=True),
                            inc="dv")
                        MXD = pl.val("dv")
                        pl.wait("t", "dv", MXD)
                        pl.wait("s", "dv", MXD)
                        pl.add("s", lambda e: e.activation(pbf[:], scs[:], AF.Exp, bias=mxc[:, 0:1],
                                                           accum_out=rsc[:, 0:1]), inc="ac")
                        EXPD = pl.val("ac")
                        pl.wait("v", "ac", EXPD)
                        pl.add("v", lambda e: e.reciprocal(rsc[:, 1:2], rsc[:, 0:1]), inc="rv")
                        pl.wait("s", "rv")
                        pl.add("s", lambda e: e.activation(pbf[:], pbf[:], AF.Copy, scale=rsc[:, 1:2]),
                               inc="ac")
                        PSC = pl.val("ac")
                        pl.wait("t", "ac", PSC)
                        for b_ in range(5):
                            pl.add("t", lambda e, b_=b_: e.transpose(
                                ps_pT[:, 128 * b_:128 * (b_ + 1)], pbf[:, 128 * b_:128 * (b_ + 1)],
                                eye[:]), inc=("pe" if b_ == 4 else None))
                        pv = pl.val("pe")
                        pl.wait("v", "pe", pv)
                        pl.drain("v")
                        pl.add("v", lambda e: e.tensor_copy(
                            pTs[:].rearrange("p b i -> p (b i)"), ps_pT[:]), inc="dv")
                        PTD = pl.val("dv")
                        pl.wait("t", "dv", PTD)
                        for b_ in range(5):
                            qb = s + b_
                            vsl, vj = (vprev, qb) if qb < 4 else (vcur, qb - 4)
                            pl.add("t", lambda e, b_=b_, vsl=vsl, vj=vj, h=h, hb64=hb64, hc=hc, oslot=oslot:
                                   e.matmul(ps_o[hb64:hb64 + 64, oslot, :],
                                            vsl[:, vj, hc, 64 * (h % 2):64 * (h % 2) + 64],
                                            pTs[:, b_, :],
                                            start=(b_ == 0), stop=(b_ == 4)),
                                   inc=("pe" if b_ == 4 else None))
                        pv = pl.val("pe")
                        pl.wait("v", "pe", pv)
                        pl.drain("v")
                        pl.add("v", lambda e, hb64=hb64, hc=hc, s=s, oslot=oslot: e.tensor_copy(
                            obf[hb64:hb64 + 64, hc, 128 * s:128 * s + 128],
                            ps_o[hb64:hb64 + 64, oslot, :]), inc="dv")
                        if h % 2 == 1:
                            oslot = 1 - oslot
                        pl.wait("t", "dv")
                OBD = pl.val("dv")
            if w > 0 and _stage in ("wo", "attn", "full"):
                if _stage == "wo":
                    OBD = pl.val("dv")
                pl.wait("t", "dv", OBD)
                for j in range(NCH):
                    ps, pv, idx = mq.group([((lambda p: p[:, :]), wo[:, k, 128 * j:128 * (j + 1)],
                                             obf[:, k, :]) for k in range(4)])
                    pl.wait("v", "pe", pv)
                    pl.drain("v")
                    pl.add("v", lambda e, ps=ps, j=j: e.tensor_tensor(
                        zf[:, j, :], zf[:, j, :], ps[:, :], ALU.add), inc="dv")
                    mq.done(idx, "dv", pl.val("dv"))
                XMD = pl.val("dv")
                snm, sval = stfree
                if snm:
                    pl.wait("y", snm, sval)
                pl.wait("y", "dv", XMD)
                pl.add("y", lambda e, w=w: e.dma_start(
                    out=xm_h[:, :, (w - 1) * T:w * T], in_=zf[:]), inc="st", n=16)
                stfree = ("st", pl.val("st"))
                pl.wait("v", "st", pl.val("st"))
            elif w > 0:
                snm, sval = stfree
                if snm:
                    pl.wait("y", snm, sval)
                pl.wait("y", "dv", pl.val("dv"))
                pl.add("y", lambda e, w=w: e.dma_start(
                    out=xm_h[:, :, (w - 1) * T:w * T], in_=zf[:]), inc="st", n=16)
                stfree = ("st", pl.val("st"))
                pl.wait("v", "st", pl.val("st"))

        pl.wait("y", "st")
        pl.build()
    return nc


def rotary_tables(t0, t1):
    DH = 64
    half = np.arange(0, DH, 2, dtype=np.float32)
    inv = (1.0 / (10000.0 ** (half / DH))).astype(np.float32)
    t = np.arange(t0, t1, dtype=np.float32)
    fr = t[:, None] * inv[None, :]
    fr = np.concatenate([fr, fr], 1)
    base = ((half + 0.4 * DH) / (1.4 * DH)).astype(np.float32)
    power = (t - SEQ // 2) / float(WIN // 2)
    scale = base[None, :] ** power[:, None]
    scale = np.concatenate([scale, scale], 1)
    cos, sin = np.cos(fr), np.sin(fr)
    sign = np.concatenate([-np.ones(32, np.float32), np.ones(32, np.float32)])
    csq = (8.0 * cos * scale).T
    ssq = (8.0 * sin * scale * sign[:, None].T).T
    csk = (cos / scale).T
    ssk = (sin / scale * sign[:, None].T).T
    out = []
    for tab in (csq, ssq, csk, ssk):
        out.append(np.ascontiguousarray(np.tile(tab, (2, 1)).astype(np.float32)))
    return out


def band_masks(batch_start):
    r = np.arange(128)[:, None]
    c = np.arange(640)[None, :]
    reg = np.where((c >= r) & (c <= r + 512), 0.0, -1e38).astype(np.float32)
    m = np.zeros((128, 5, 640), np.float32)
    m[:, 0] = reg
    for s in range(4):
        v = reg.copy()
        if batch_start:
            v[:, :max(0, 512 - 128 * s)] = -1e38
        m[:, 1 + s] = v
    return m.astype(BF)


def lattn_inputs_geom(inputs, cores, TOK, layer, pre_ln, xin_list):
    wqkv = wchunks(np.asarray(inputs["L0_ffn_wqkv"])[layer])
    wo = wchunks(np.asarray(inputs["L0_ffn_wo"])[layer])
    qs = np.asarray(inputs["L0_ffn_q_scale"])[layer].astype(np.float32)
    ks = np.asarray(inputs["L0_ffn_k_scale"])[layer].astype(np.float32)
    bq = np.zeros((2, 128), np.float32)
    bq[0, 0:64] = qs
    bq[1, 64:128] = qs
    bk = np.zeros((2, 128), np.float32)
    bk[0, 0:64] = ks
    bk[1, 64:128] = ks
    eye = np.eye(128, dtype=np.float32)
    eye2 = np.concatenate([np.eye(64), np.eye(64)], 0).astype(np.float32)
    colsv = np.zeros((128, LA_NCOL), np.float32)
    colsv[:, 0] = EPS
    colsv[:, 1] = -0.5
    colsv[:, 2] = 1e-24
    maps = []
    for (b, a), xin in zip(cores, xin_list):
        tabs = rotary_tables(a - WIN, a + TOK)
        maps.append(dict(x=xin, wqkv=wqkv, wo=wo, bq=bq, bk=bk,
                         csq=tabs[0], ssq=tabs[1], csk=tabs[2], ssk=tabs[3],
                         msk=band_masks(a == 0), eye=eye, eye2=eye2, cols=colsv))
    return maps


LF_COLS = {}
_c = 0
for _nm, _w in [("lng", 8), ("lnb", 8), ("b1a", 22), ("b1g", 22), ("b2", 8),
                ("kEPS", 1), ("kM05", 1)]:
    LF_COLS[_nm] = _c
    _c += _w
LF_NCOL = _c
NHC = 22


def build_lffn(TOK, final, T=256):
    NT = TOK // T
    nc = bass.Bass("TRN2", target_bir_lowering=False)
    xm_h = nc.dram_tensor("xm", [128, NCH, 1 + TOK], F32, kind="ExternalInput").ap()
    w1a_h = nc.dram_tensor("w1a", [128, NCH, NHC * 128], F32, kind="ExternalInput").ap()
    w1g_h = nc.dram_tensor("w1g", [128, NCH, NHC * 128], F32, kind="ExternalInput").ap()
    w2_h = nc.dram_tensor("w2", [128, NHC, D], F32, kind="ExternalInput").ap()
    cols_h = nc.dram_tensor("cols", [128, LF_NCOL], F32, kind="ExternalInput").ap()
    if final:
        x1_h = nc.dram_tensor("x1in", [128, NCH, TOK], F32, kind="ExternalInput").ap()
    zo_h = nc.dram_tensor("zout", [128, NCH, TOK], F32, kind="ExternalOutput").ap()

    with contextlib.ExitStack() as st:
        def sb(name, shape, dtype):
            return st.enter_context(nc.sbuf_tensor(name, shape, dtype))

        def pt(name, shape):
            return st.enter_context(nc.psum_tensor(name, shape, F32))

        w1a = sb("w1a_s", [128, NCH, NHC * 128], BF16)
        w1g = sb("w1g_s", [128, NCH, NHC * 128], BF16)
        w2 = sb("w2_s", [128, NHC, D], BF16)
        cols = sb("cols_s", [128, LF_NCOL], F32)
        ones_c = sb("ones_c", [128, 1], BF16)
        ones_r = sb("ones_r", [1, 128], BF16)
        xsl = [sb(f"xm{i}", [128, NCH, T + 1], F32) for i in range(2)]
        x1sl = [sb("x1s0", [128, NCH, T], F32)] if final else None
        cbf = sb("cbf", [128, NCH, T], BF16)
        sqbf = sb("sqbf", [128, NCH, T], BF16)
        hf = sb("hf", [128, NCH, T], F32)
        hb = sb("hb", [128, NCH, T], BF16)
        gl = sb("gl", [128, T], F32)
        h1 = sb("h1", [128, NHC, T], BF16)
        zo = sb("zo", [128, NCH, T], F32)
        rowa = sb("rowa", [1, 4 * T], F32)
        rowhl = sb("rowhl", [1, 2 * T], BF16)

        ps_stat = pt("ps_stat", [1, 2 * T])
        ps_ab = pt("ps_ab", [128, 2 * T])
        ps_mm = [pt(f"ps_mm{i}", [128, T]) for i in range(3)]

        pl = Plan(nc, st)

        def col(name, c=0):
            i = LF_COLS[name] + c
            return cols[:, i:i + 1]

        kEPS = cols[0:1, LF_COLS["kEPS"]:LF_COLS["kEPS"] + 1]
        kM05 = cols[0:1, LF_COLS["kM05"]:LF_COLS["kM05"] + 1]

        for (dst, srch) in [(w1a, w1a_h), (w1g, w1g_h), (w2, w2_h), (cols, cols_h)]:
            pl.add("g", lambda e, dst=dst, srch=srch: e.dma_start(out=dst[:], in_=srch[:]),
                   inc="wld", n=16)
        WLD = pl.val("wld")
        for eng in ("v", "s", "t"):
            pl.wait(eng, "wld", WLD)
        pl.add("v", lambda e: e.memset(ones_c[:], 1.0))
        pl.drain("v")
        pl.add("v", lambda e: e.memset(ones_r[:], 1.0), inc="dv")

        mq = MMQ(pl, ps_mm)
        stfree = ("", 0)
        xfree = [("", 0), ("", 0)]
        x1free = [("", 0)]

        def load(i):
            sl = i % 2
            snm, sval = xfree[sl]
            if snm:
                pl.wait("y", snm, sval)
            pl.add("y", lambda e, i=i, sl=sl: e.dma_start(
                out=xsl[sl][:], in_=xm_h[:, :, i * T:(i + 1) * T + 1]), inc=f"x{sl}", n=16)


        load(0)

        def hslice(xt, c):
            return xt[:, c, 1:T + 1] if c < 4 else xt[:, c, 0:T]

        for i in range(NT):
            sl = i % 2
            xt = xsl[sl]
            pl.wait("v", f"x{sl}")
            pl.wait("s", f"x{sl}")
            pl.drain("v")
            for c in range(NCH):
                pl.add("v", lambda e, xt=xt, c=c: e.tensor_copy(cbf[:, c, :], hslice(xt, c)),
                       inc=("dv" if c == NCH - 1 else None))
            pl.drain("s")
            for c in range(NCH):
                pl.add("s", lambda e, xt=xt, c=c: e.activation(sqbf[:, c, :], hslice(xt, c), AF.Square),
                       inc=("ac" if c == NCH - 1 else None))
            pl.wait("t", "dv")
            pl.wait("t", "ac")
            for c in range(NCH):
                pl.add("t", lambda e, c=c: e.matmul(ps_stat[0:1, 0:T], ones_c[:], cbf[:, c, :],
                                                    start=(c == 0), stop=(c == NCH - 1)))
            for c in range(NCH):
                pl.add("t", lambda e, c=c: e.matmul(ps_stat[0:1, T:2 * T], ones_c[:], sqbf[:, c, :],
                                                    start=(c == 0), stop=(c == NCH - 1)),
                       inc=("pe" if c == NCH - 1 else None))
            pl.wait("v", "pe")
            alpha, beta = emit_ln_rows(pl, ps_stat, rowa, T, T, double=False, tag="r",
                                       kEPS=kEPS, kM05=kM05)
            emit_ab_bcast(pl, rowhl, alpha, beta, ones_r, ps_ab, T, T, tag="b")
            for c in range(NCH):
                pl.add("v", lambda e, xt=xt, c=c: e.tensor_tensor(
                    hf[:, c, :], hslice(xt, c), ps_ab[:, 0:T], ALU.mult))
            pl.drain("v")
            for c in range(NCH):
                pl.add("v", lambda e, c=c: e.tensor_tensor(
                    hf[:, c, :], hf[:, c, :], ps_ab[:, T:2 * T], ALU.subtract))
            pl.drain("v")
            for c in range(NCH):
                pl.add("v", lambda e, c=c: e.tensor_scalar(
                    hb[:, c, :], hf[:, c, :], col("lng", c), col("lnb", c), ALU.mult, ALU.add),
                    inc=("dv" if c == NCH - 1 else None))
            HBD = pl.val("dv")
            if i + 1 < NT:
                load(i + 1)
            pl.wait("t", "dv", HBD)
            for j in range(NHC):
                psg, pvg, idxg = mq.group([((lambda p: p[:, :]),
                                            w1g[:, k, 128 * j:128 * (j + 1)], hb[:, k, :])
                                           for k in range(NCH)])
                psa, pva, idxa = mq.group([((lambda p: p[:, :]),
                                            w1a[:, k, 128 * j:128 * (j + 1)], hb[:, k, :])
                                           for k in range(NCH)])
                pl.wait("s", "pe", pvg)
                pl.wait("s", "dv", pl.val("dv"))
                pl.add("s", lambda e, psg=psg, j=j: e.activation(
                    gl[:], psg[:, :], AF.Gelu, bias=col("b1g", j)), inc="ac")
                mq.done(idxg, "ac", pl.val("ac"))
                pl.wait("v", "pe", pva)
                pl.wait("v", "ac")
                pl.add("v", lambda e, psa=psa, j=j: e.scalar_tensor_tensor(
                    h1[:, j, :], psa[:, :], col("b1a", j), gl[:], ALU.add, ALU.mult),
                    inc="dv")
                mq.done(idxa, "dv", pl.val("dv"))
            pl.wait("t", "dv")
            for j in range(NCH):
                ps, pv, idx = mq.group([((lambda p: p[:, :]),
                                         w2[:, k, 128 * j:128 * (j + 1)], h1[:, k, :])
                                        for k in range(NHC)])
                pl.wait("v", "pe", pv)
                pl.drain("v")
                pl.add("v", lambda e, ps=ps, xt=xt, j=j: e.scalar_tensor_tensor(
                    zo[:, j, :], ps[:, :], col("b2", j), xt[:, j, 1:T + 1], ALU.add, ALU.add),
                    inc="dv")
                mq.done(idx, "dv", pl.val("dv"))
            if final:
                x1t = x1sl[0]
                snm1, sval1 = x1free[0]
                if snm1:
                    pl.wait("y", snm1, sval1)
                pl.add("y", lambda e, i=i: e.dma_start(
                    out=x1t[:], in_=x1_h[:, :, i * T:(i + 1) * T]), inc="x1ld", n=16)
                pl.wait("v", "x1ld")
                pl.drain("v")
                for j in range(NCH):
                    pl.add("v", lambda e, x1t=x1t, j=j: e.tensor_tensor(
                        zo[:, j, :], zo[:, j, :], x1t[:, j, :], ALU.add),
                        inc=("dv" if j == NCH - 1 else None))
                x1free[0] = ("dv", pl.val("dv"))
            ZOD = pl.val("dv")
            snm, sval = stfree
            if snm:
                pl.wait("y", snm, sval)
            pl.wait("y", "dv", ZOD)
            pl.add("y", lambda e, i=i: e.dma_start(out=zo_h[:, :, i * T:(i + 1) * T], in_=zo[:]),
                   inc="st", n=16)
            stfree = ("st", pl.val("st"))
            pl.wait("v", "st", pl.val("st"))
        pl.wait("y", "st")
        pl.build()
    return nc


def lffn_inputs_geom(inputs, cores, TOK, layer, final, xm_list, x1_list=None):
    HID = 2730
    w1 = np.asarray(inputs["L0_ffn_w1"])[layer]
    b1 = np.asarray(inputs["L0_ffn_b1"])[layer]
    w2 = np.asarray(inputs["L0_ffn_w2"])[layer]
    b2 = np.asarray(inputs["L0_ffn_b2"])[layer]
    lng = np.asarray(inputs["L0_ffn_ln_g"])[layer]
    lnb = np.asarray(inputs["L0_ffn_ln_b"])[layer]
    w1a = np.zeros((1024, NHC * 128), np.float32)
    w1a[:, :HID] = w1[:, :HID]
    w1g = np.zeros((1024, NHC * 128), np.float32)
    w1g[:, :HID] = w1[:, HID:]
    w2p = np.zeros((NHC * 128, 1024), np.float32)
    w2p[:HID] = w2
    colsv = np.zeros((128, LF_NCOL), np.float32)
    colsv[:, LF_COLS["lng"]:LF_COLS["lng"] + 8] = colvec(lng)
    colsv[:, LF_COLS["lnb"]:LF_COLS["lnb"] + 8] = colvec(lnb)
    b1a = np.zeros(NHC * 128, np.float32)
    b1a[:HID] = b1[:HID]
    b1g = np.zeros(NHC * 128, np.float32)
    b1g[:HID] = b1[HID:]
    colsv[:, LF_COLS["b1a"]:LF_COLS["b1a"] + NHC] = b1a.reshape(NHC, 128).T
    colsv[:, LF_COLS["b1g"]:LF_COLS["b1g"] + NHC] = b1g.reshape(NHC, 128).T
    colsv[:, LF_COLS["b2"]:LF_COLS["b2"] + 8] = colvec(b2)
    colsv[:, LF_COLS["kEPS"]] = EPS
    colsv[:, LF_COLS["kM05"]] = -0.5
    w1ac = wchunks(w1a)
    w1gc = wchunks(w1g)
    w2c = wchunks(w2p)
    maps = []
    for ci, (b, a) in enumerate(cores):
        m = dict(xm=xm_list[ci], w1a=w1ac, w1g=w1gc, w2=w2c, cols=colsv)
        if final:
            m["x1in"] = x1_list[ci]
        maps.append(m)
    return maps
